# version 01 (final)
# baseline (speedup 1.0000x reference)
import numpy as np
import concourse.bass as bass
import concourse.mybir as mybir

F32 = mybir.dt.float32
BF16 = mybir.dt.bfloat16
I32 = mybir.dt.int32
AF = mybir.ActivationFunctionType
ALU = mybir.AluOpType
AX = mybir.AxisListType

ENGINES = ("sync", "gpsimd", "tensor", "scalar", "vector")
S, D, NT, NJ, NC, KB = 2048, 1024, 16, 4, 8, 16


class Builder:
    def __init__(self, nc):
        self.nc = nc
        self.plan = {e: [] for e in ENGINES}
        self.cnt = {}
        self.owner = {}
        self.lastw = {e: {} for e in ENGINES}
        self.semh = {}

    def sem(self, name, owner):
        self.cnt[name] = 0
        self.owner[name] = owner
        self.semh[name] = self.nc.alloc_semaphore(name)

    def op(self, eng, fn, waits=(), inc=None, n=1):
        w = []
        for sv in waits:
            if sv is None:
                continue
            s, v = sv
            if v is None or v <= 0 or self.owner[s] == eng:
                continue
            if self.lastw[eng].get(s, 0) >= v:
                continue
            self.lastw[eng][s] = v
            w.append((s, v))
        if inc is not None:
            self.cnt[inc] += n
        self.plan[eng].append((w, fn, inc, n))
        return (inc, self.cnt[inc]) if inc is not None else None

    def dma(self, queue, out, in_, waits=(), slow=False):
        semn = "w" if queue == "gpsimd" else "xq"
        return self.op(queue,
                       lambda e, o=out, i=in_, sl=slow: e.dma_start(
                           out=o, in_=i, allow_slow_non_contiguous=sl),
                       waits=waits, inc=semn, n=16)

    def force_wait(self, eng, semname, val):
        self.plan[eng].append(([(semname, val)], None, None, 0))

    def now(self, sem):
        return (sem, self.cnt[sem])

    def barrier_counts(self):
        return [self.now(s) for s in ("t", "s", "v", "g")]

    def emit(self):
        with self.nc.Block() as blk:
            for name in ENGINES:
                def body(eng, n=name):
                    for w, fn, inc, amt in self.plan[n]:
                        for s, v in w:
                            eng.wait_ge(self.semh[s], v)
                        if fn is None:
                            continue
                        ins = fn(eng)
                        if inc is not None:
                            ins.then_inc(self.semh[inc], amt)
                getattr(blk, name)(body)


class Slot:
    def __init__(self, ap):
        self.ap = ap
        self.cons = None
        self.prod = None


def build(nc, limit="all", dbg=(), only_e=None):
    dbg = set(dbg)
    b = Builder(nc)
    for name, owner in (("w", "dma"), ("xq", "dma"), ("g", "gpsimd"),
                        ("t", "tensor"), ("s", "scalar"), ("v", "vector")):
        b.sem(name, owner)

    dr = {}
    def din(name, shape, dt=F32):
        dr[name] = nc.dram_tensor(name, list(shape), dt, kind="ExternalInput").ap()
    din("x_in", (S, D))
    din("l0_wq", (D, D)); din("l0_bq", (D,)); din("l0_wo", (D, D)); din("l0_bo", (D,))
    din("l0_wr", (D, 8)); din("l0_br", (8,))
    din("l0_w1", (8, D, 2048)); din("l0_b1", (8, 2048))
    din("l0_w2", (8, 2048, D)); din("l0_b2", (8, D))
    din("l1_qkvw", (3 * D, D)); din("l1_qkvb", (3 * D,))
    din("l1_outw", (D, D)); din("l1_outb", (D,))
    din("l1_wg", (3072, D), I32); din("l1_wu", (3072, D), I32)
    din("l1_wd", (D, 3072), I32)
    y_out = nc.dram_tensor("y_out", [S, D], F32, kind="ExternalOutput").ap()
    dbt = {}
    for name, width in (("hT", NC * S), ("hv", KB * 16 * 65), ("valsT", NC * 512),
                        ("lnT", NC * S), ("maskp", 128), ("hTl1", 22 * 1024),
                        ("qTj", NC * 512), ("fpk", 256), ("valsT0", NC * 512),
                        ("bpkd", 2624), ("PTd", 2 * 2 * 2 * 512),
                        ("vu0", NC * 512), ("vu1", NC * 512),
                        ("vu2", NC * 512), ("PT0", 4096), ("bpk0", 2624),
                        ("hTeA", 10 * S), ("hTeC", 6 * S),
                        ("xaf0", NT * D), ("xaf1", NT * D), ("xaf3", NT * D),
                        ("xaf5", NT * D), ("xaf6", NT * D), ("xaf7", NT * D),
                        ("mkpT", 2048), ("mkpT2", 2048)):
        if name in dbg:
            dbt[name] = nc.dram_tensor("dbg_" + name, [128, width], F32,
                                       kind="ExternalOutput").ap()

    BASE = 20608 + 2048
    o_X = BASE
    o_G = o_X + 65536
    o_B = o_G + 10624
    o_C = o_B + 32768
    o_ATT = o_C + 53248
    assert o_ATT + 41472 + 2048 + 96 <= 229344

    sb = {}
    def alloc(name, shape, dt, offset):
        sb[name] = nc.alloc_sbuf_tensor_at(name, list(shape), dt, offset=offset).ap()
        return sb[name]

    x = alloc("x", [128, NT, D], F32, o_X)
    g = o_G
    ones = alloc("ones", [128, 128], BF16, g); g += 256
    iden = alloc("iden", [128, 128], BF16, g); g += 256
    maskM = alloc("maskM", [128, 896], BF16, g); g += 1792
    fpk = alloc("fpk", [128, 256], F32, g); g += 1024
    bpk = alloc("bpk", [128, 2624], BF16, g); g += 5248
    lnb = alloc("lnb", [128, 1024], BF16, g)
    maskp = alloc("maskpf", [128, 128], F32, g)
    den = alloc("den", [128, 512], F32, g)
    g += 2048
    assert g <= o_G + 10624

    lnT = alloc("lnT", [128, NC, S], BF16, o_B)

    hT = alloc("hT", [128, NC, S], BF16, o_C)
    kT = hT
    qTj = alloc("qTj", [128, NC, 512], BF16, o_C + 32768)
    valsT = alloc("valsT", [128, NC, 512], BF16, o_C + 40960)
    wldb = alloc("wld", [128, 1024], BF16, o_C + 49152)
    wtrb = alloc("wtr", [128, 1024], BF16, o_C + 51200)
    w2sl = alloc("w2sl", [128, 18, 512], BF16, o_C)
    w1sl = alloc("w1sl", [128, 2, 8, 128], BF16, o_C + 18432)
    maskpT = alloc("maskpT", [8, S], BF16, o_C + 22528)
    hTeC = alloc("hTeC", [128, 6, S], BF16, o_C + 26624)
    wdsl = alloc("wdsl", [128, 26, 512], BF16, o_C)
    silu = alloc("silu", [128, 2, 512], F32, o_C + 26624)
    guld = alloc("guld", [128, 2, 1024], BF16, o_C + 30720)
    gutr = alloc("gutr", [128, 2, 1024], BF16, o_C + 34816)

    ytmp = alloc("ytmp", [128, 2, 512], BF16, o_ATT + 41472)
    denR = alloc("denR", [2, 512], F32, o_ATT + 41472)
    sp32 = alloc("sp32", [128, 33], BF16, o_ATT + 41472 + 2048)
    qnstg = alloc("qnstg", [128, 2, 512], BF16, o_ATT + 41472)
    hv = alloc("hv", [128, KB, 16 * 65], BF16, o_ATT)
    hvh = hv.rearrange("p k (h c) -> p k h c", h=16)
    PT = alloc("PT", [128, 2, 2, 2, 512], BF16, o_ATT + 33280)
    hTeA = alloc("hTeA", [128, 10, S], BF16, o_ATT)
    hTl1A = alloc("hTl1A", [128, 20, 1024], BF16, o_ATT)
    hTl1C = alloc("hTl1C", [128, 4, 1024], BF16, o_C + 38912)

    bcast_sb = bpk[:, 2112:2624]
    recipb_r = lambda hh: (bpk[64:65, 1024:1536] if hh == 0
                           else bpk[32:33, 1024:1536])
    bqT = fpk[:, 0:8]
    qkvbT = fpk[:, 8:32]
    negm_c = lambda t: fpk[:, 32 + t:33 + t]
    ssq_c = lambda t: fpk[:, 48 + t:49 + t]
    rstd_c = lambda t: fpk[:, 64 + t:65 + t]
    b1T = fpk[:, 96:224]
    rmax_c = fpk[:, 224:225]
    rden_c = fpk[:, 225:226]
    rexp_c = fpk[:, 226:234]
    rrec_c = fpk[:, 234:235]
    raws_c = lambda t: fpk[:, 240 + t:241 + t]
    rstdB_c = lambda t: fpk[:, 80 + t:81 + t]
    bo_row = bpk[0:1, 0:1024]
    outb_row = bpk[32:33, 0:1024]
    br_row = bpk[64:65, 0:8]
    b2_rows = bpk[0:8, 1024:2048]
    wr_t = bpk[:, 2048:2112].rearrange("p (c e) -> p c e", c=8)

    PAt = nc.alloc_psum_tensor("PA", [128, 1024], F32).ap()
    PBt = nc.alloc_psum_tensor("PB", [128, 1024], F32).ap()
    H = [Slot(PAt[:, 0:512]), Slot(PAt[:, 512:1024]),
         Slot(PBt[:, 0:512]), Slot(PBt[:, 512:1024])]
    PC = Slot(nc.alloc_psum_tensor("PC", [128, 512], F32).ap())
    PD = Slot(nc.alloc_psum_tensor("PD", [128, 512], F32).ap())
    PE = Slot(nc.alloc_psum_tensor("PE", [128, 512], F32).ap())
    PTB = Slot(nc.alloc_psum_tensor("PTB", [128, 1024], BF16).ap())

    def mm(slot, lhsT, rhs, start, stop, waits=(), tp=None, out=None):
        o = slot.ap if out is None else out
        r = b.op("tensor",
                 lambda e, o=o, l=lhsT, rr=rhs, st=start, sp=stop, t=tp:
                 e.matmul(o, l, rr, start=st, stop=sp, tile_position=t),
                 waits=waits, inc="t")
        slot.prod = r
        return r

    def mm_group(slot, mms, waits=(), open_grp=True, close_grp=True, out=None):
        n = len(mms)
        r = None
        for i, (lhsT, rhs, tp) in enumerate(mms):
            w = (list(waits) + ([slot.cons] if open_grp else [])) if i == 0 else []
            r = mm(slot, lhsT, rhs, start=(i == 0 and open_grp),
                   stop=(i == n - 1 and close_grp), waits=w, tp=tp, out=out)
        return r

    def trs(trlist, waits=()):
        n = len(trlist)
        r = None
        for i, (o, in_) in enumerate(trlist):
            w = (list(waits) + [PTB.cons, ("g", 2)]) if i == 0 else []
            r = b.op("tensor", lambda e, oo=o, ii=in_: e.transpose(oo, ii, iden[:, :]),
                     waits=w, inc="t")
        PTB.prod = r
        return r

    def evac(eng, slot, fn, waits=(), slots_extra=()):
        sem = "s" if eng == "scalar" else "v"
        allw = list(waits) + [slot.prod] + [s.prod for s in slots_extra]
        r = b.op(eng, fn, waits=allw, inc=sem)
        slot.cons = r
        for s in slots_extra:
            s.cons = r
        return r

    c_x = b.dma("sync", x[:, :, :], dr["x_in"].rearrange("(t p) d -> p t d", p=128))
    b.dma("sync", bqT, dr["l0_bq"].rearrange("(c p) -> p c", p=128), slow=True)
    b.dma("sync", qkvbT, dr["l1_qkvb"].rearrange("(c p) -> p c", p=128), slow=True)
    for jc in range(16):
        c_fpk = b.dma("sync", fpk[:, 96 + jc * 8:96 + (jc + 1) * 8],
                      dr["l0_b1"][:, jc * 128:(jc + 1) * 128].rearrange(
                          "e p -> p e"), slow=True)

    b.op("gpsimd", lambda e: e.memset(ones[:, :], 1.0), inc="g")
    b.op("gpsimd", lambda e: e.affine_select(
        out=iden[:, :], in_=ones[:, :], pattern=[[-1, 128]],
        compare_op=ALU.is_equal, fill=0.0, base=0, channel_multiplier=1), inc="g")
    b.op("gpsimd", lambda e: e.memset(maskM[:, :], 1.0), inc="g")
    b.op("gpsimd", lambda e: e.affine_select(
        out=maskM[:, :], in_=maskM[:, :], pattern=[[1, 896]],
        compare_op=ALU.is_ge, fill=0.0, base=-384, channel_multiplier=-1), inc="g")
    c_hvones = b.op("gpsimd", lambda e: e.memset(hvh[:, :, :, 64:65], 1.0), inc="g")
    b.op("gpsimd", lambda e: e.memset(sp32[:, :], 0.0), inc="g")
    b.op("gpsimd", lambda e: e.memset(sp32[0:64, 0:1], 1.0), inc="g")
    c_sp32 = b.op("gpsimd", lambda e: e.memset(sp32[64:128, 32:33], 1.0), inc="g")

    c_bo = b.dma("gpsimd", bo_row, dr["l0_bo"][None, :])
    c_outb = b.dma("gpsimd", outb_row, dr["l1_outb"][None, :])
    c_br = b.dma("gpsimd", br_row, dr["l0_br"][None, :])
    c_b2 = b.dma("gpsimd", b2_rows, dr["l0_b2"][:, :])
    c_wr = b.dma("gpsimd", wr_t, dr["l0_wr"].rearrange("(c p) e -> p c e", p=128))

    ready_x = [("xq", c_x[1])] * NT
    b.op("vector", lambda e: e.memset(fpk[:, 227:229], 0.0), inc="v")
    c_ebias = b.op("vector", lambda e: e.memset(fpk[:, 229:230], 30.0), inc="v")

    def gapv():
        b.op("vector", lambda e: e.tensor_copy(out=fpk[:, 228:229],
                                               in_=fpk[:, 227:228]), inc="v")

    lnb_free = [None]

    def ln_phase(guard_s=None, guard_v=None):
        ready = []
        for t in range(NT):
            rw = [ready_x[t]]
            b.op("vector", lambda e, t=t: e.reduce_sum(
                out=raws_c(t), in_=x[:, t, :], axis=AX.X, negate=True),
                waits=rw + [guard_v], inc="v")
            gapv()
            c2 = b.op("vector", lambda e, t=t: e.tensor_scalar(
                out=negm_c(t), in0=raws_c(t), scalar1=1.0 / D, scalar2=None,
                op0=ALU.mult), inc="v")
            c3 = b.op("scalar", lambda e, t=t: e.activation(
                out=lnb[:, :], in_=x[:, t, :], func=AF.Square,
                bias=negm_c(t), scale=1.0, accum_out=ssq_c(t)),
                waits=[("v", c2[1]), c_fpk, guard_s, lnb_free[0]] + rw, inc="s")
            c4 = b.op("vector", lambda e, t=t: e.tensor_scalar(
                out=rstd_c(t), in0=ssq_c(t), scalar1=1.0 / D, scalar2=1e-5,
                op0=ALU.mult, op1=ALU.add), waits=[("s", c3[1])], inc="v")
            c5 = b.op("scalar", lambda e, t=t: e.activation(
                out=rstd_c(t), in_=rstd_c(t), func=AF.Sqrt),
                waits=[("v", c4[1])], inc="s")
            c6 = b.op("vector", lambda e, t=t: e.reciprocal(
                out=rstdB_c(t), in_=rstd_c(t)), waits=[("s", c5[1])], inc="v")
            gapv()
            c7 = b.op("vector", lambda e, t=t: e.tensor_scalar(
                out=lnb[:, :], in0=x[:, t, :], scalar1=negm_c(t), scalar2=rstdB_c(t),
                op0=ALU.add, op1=ALU.mult),
                waits=[("v", c6[1]), ("s", c3[1])], inc="v")
            rt = trs([(PTB.ap[:, ic * 128:(ic + 1) * 128],
                  lnb[:, ic * 128:(ic + 1) * 128]) for ic in range(NC)],
                waits=[("v", c7[1])])
            lnb_free[0] = ("t", rt[1])
            ev = evac("vector", PTB, lambda e, t=t: e.tensor_copy(
                out=lnT[:, :, t * 128:(t + 1) * 128], in_=PTB.ap[:, :]))
            ready.append(ev)
        return [("v", ready[4 * jj + 3][1]) for jj in range(NJ)]

    wstate = {"cons": None, "wtr_cons": None}

    def w_slice(dram_slice_ap, extra=None):
        return b.dma("gpsimd", wldb[:, :], dram_slice_ap,
                     waits=[wstate["cons"], extra])

    def w_tr(c_load):
        r = trs([(PTB.ap[:, i * 128:(i + 1) * 128],
                  wldb[:, i * 128:(i + 1) * 128]) for i in range(NC)],
                waits=[c_load])
        wstate["cons"] = ("t", r[1])
        ev = evac("vector", PTB,
                  lambda e: e.tensor_copy(out=wtrb[:, :], in_=PTB.ap[:, :]),
                  waits=[wstate["wtr_cons"]])
        return ev

    def wtr_consumed():
        wstate["wtr_cons"] = b.now("t")

    def proj_to(dram_w, oc_list, lnr, dest_fn, bias_ap_fn, jsel=None, extra=None):
        jsel = list(range(NJ)) if jsel is None else jsel
        res = []
        for oc in oc_list:
            cl = w_slice(dram_w[oc * 128:(oc + 1) * 128, :], extra=extra)
            cw = w_tr(cl)
            for idx, jj in enumerate(jsel):
                slot = H[(oc + idx) % 4]
                mm_group(slot, [(wtrb[:, dc * 128:(dc + 1) * 128],
                                 lnT[:, dc, jj * 512:(jj + 1) * 512], None)
                                for dc in range(NC)],
                         waits=[cw, lnr[jj]])
                dest, eng = dest_fn(oc, jj)
                r = evac(eng, slot,
                         (lambda e, d=dest, sl=slot, bc=bias_ap_fn(oc):
                          e.activation(out=d, in_=sl.ap, func=AF.Identity,
                                       bias=bc, scale=1.0)))
                res.append((oc, jj, r))
            wtr_consumed()
        return res

    ptstate = [None, None]
    vals_state = [None]
    qn_state = {"war": None, "sq": None}

    def attn_prelude(li, c_kv):
        cm0 = b.op("vector", lambda e: e.memset(fpk[0:33, 236:245], 0.0),
                   waits=[c_kv], inc="v")
        for hp in range(8):
            for t5 in range(NJ):
                csq = b.op("vector", lambda e, hp=hp, t5=t5: e.tensor_tensor(
                    out=qnstg[:, 0, :], in0=kT[:, hp, t5 * 512:(t5 + 1) * 512],
                    in1=kT[:, hp, t5 * 512:(t5 + 1) * 512], op=ALU.mult),
                    waits=[c_kv, qn_state["sq"], ("v", cm0[1])], inc="v")
                r = mm(H[2], sp32[:, :], qnstg[:, 0, :], start=True,
                       stop=True, waits=[H[2].cons, ("v", csq[1]), c_sp32],
                       out=H[2].ap[0:33, :])
                qn_state["sq"] = r
                cr = b.op("vector", lambda e: e.reduce_max(
                    out=fpk[0:33, 245:246], in_=H[2].ap[0:33, :], axis=AX.X),
                    waits=[r], inc="v")
                H[2].cons = cr
                gapv()
                b.op("vector", lambda e, hp=hp: e.tensor_max(
                    out=fpk[0:33, 236 + hp:237 + hp],
                    in0=fpk[0:33, 236 + hp:237 + hp],
                    in1=fpk[0:33, 245:246]), inc="v")
        cs = b.op("scalar", lambda e: e.activation(
            out=fpk[0:33, 236:244], in_=fpk[0:33, 236:244], func=AF.Sqrt),
            waits=[b.now("v")], inc="s")
        return ("s", cs[1])

    def attn_layer(li, lnr, c_kv, c_hv, c_mxk):
        scale = 0.125
        ebias = fpk[:, 229:230]
        for j in range(NJ):
            c_q = None
            if li == 1:
                qres = proj_to(dr["l1_qkvw"], list(range(8)), lnr,
                               lambda oc, jj: (qTj[:, oc, :], "scalar"),
                               lambda oc: qkvbT[:, oc:oc + 1], jsel=[j])
                c_q = ("s", qres[-1][2][1])
            for hp in range(8):
                qsrcT = (hT[:, hp, j * 512:(j + 1) * 512] if li == 0
                         else qTj[:, hp, :])
                csq = b.op("vector", lambda e, q_=qsrcT: e.tensor_tensor(
                    out=qnstg[:, 0, :], in0=q_, in1=q_, op=ALU.mult),
                    waits=[c_kv, c_q, qn_state["sq"]], inc="v")
                rqn = mm(H[2], sp32[:, :], qnstg[:, 0, :], start=True,
                         stop=True, waits=[H[2].cons, ("v", csq[1]), c_sp32],
                         out=H[2].ap[0:33, :])
                qn_state["sq"] = rqn
                csr = b.op("scalar", lambda e: e.activation(
                    out=qnstg[0:33, 0, :], in_=H[2].ap[0:33, :], func=AF.Sqrt),
                    waits=[rqn, qn_state["war"]], inc="s")
                qn_state["sq"] = ("s", csr[1])
                H[2].cons = csr
                cqa = b.op("vector", lambda e, hp=hp: e.tensor_scalar(
                    out=qnstg[0:1, 1, :], in0=qnstg[0:1, 0, :],
                    scalar1=fpk[0:1, 236 + hp:237 + hp], scalar2=-1.0,
                    op0=ALU.mult, op1=ALU.mult),
                    waits=[("s", csr[1]), c_mxk, qn_state["war"]], inc="v")
                cqb = b.op("vector", lambda e, hp=hp: e.tensor_scalar(
                    out=qnstg[32:33, 1, :], in0=qnstg[32:33, 0, :],
                    scalar1=fpk[32:33, 236 + hp:237 + hp], scalar2=-1.0,
                    op0=ALU.mult, op1=ALU.mult), inc="v")
                c_qn = ("v", cqb[1])
                nkb = KB if li == 0 else 4 * j + 4
                kbs = list(range(nkb))
                groups = [kbs[i:i + 2] for i in range(0, nkb, 2)]
                ng = len(groups)
                for gi, grp in enumerate(groups):
                    slp = gi % 2
                    last_s, last_v = None, None
                    for ki, kb in enumerate(grp):
                        qk = [H[0], H[1]] if (ki % 2 == 0) else [H[2], H[3]]
                        pfull = PAt if (ki % 2 == 0) else PBt
                        for hh in range(2):
                            if li == 0:
                                rq = hT[hh * 64:(hh + 1) * 64, hp,
                                        j * 512:(j + 1) * 512]
                            else:
                                rq = qTj[hh * 64:(hh + 1) * 64, hp, :]
                            mm(qk[hh],
                               kT[hh * 64:(hh + 1) * 64, hp,
                                  kb * 128:(kb + 1) * 128],
                               rq, start=True, stop=False,
                               waits=[qk[hh].cons, c_kv, c_q],
                               tp=(hh * 64, 0))
                            mm(qk[hh],
                               ones[32 * hh:32 * hh + 1, 0:128],
                               qnstg[32 * hh:32 * hh + 1, 1, :],
                               start=False, stop=True,
                               waits=[c_qn], tp=(32 * hh, 0))
                        r = evac("scalar", qk[0],
                                 (lambda e, p=pfull, slp=slp, ki=ki:
                                  e.activation(out=PT[:, slp, ki, :, :],
                                               in_=p[:, :], func=AF.Exp,
                                               bias=ebias, scale=scale)),
                                 waits=[ptstate[slp], ("v", c_ebias[1])],
                                 slots_extra=[qk[1]])
                        last_s = r
                        if li == 1 and kb >= 4 * j:
                            rr = kb - 4 * j
                            for hh in range(2):
                                last_v = b.op(
                                    "vector",
                                    lambda e, slp=slp, ki=ki, rr=rr, hh=hh:
                                    e.tensor_tensor(
                                        out=PT[:, slp, ki, hh, :],
                                        in0=PT[:, slp, ki, hh, :],
                                        in1=maskM[:, 384 - 128 * rr:
                                                  896 - 128 * rr],
                                        op=ALU.mult),
                                    waits=[("s", r[1])], inc="v")
                    pv_waits = [last_s, last_v, c_hv, c_hvones]
                    mm_group(PC,
                             [(hv[:, kb, (2 * hp) * 65:(2 * hp) * 65 + 65],
                               PT[:, slp, ki2, 0, :], None)
                              for ki2, kb in enumerate(grp)],
                             waits=pv_waits,
                             open_grp=(gi == 0), close_grp=(gi == ng - 1),
                             out=PC.ap[0:65, :])
                    n_in_g = len(grp)
                    for ki2, kb in enumerate(grp):
                        w0 = (pv_waits + ([PD.cons, PE.cons] if gi == 0 else []))\
                            if ki2 == 0 else []
                        rb = PT[:, slp, ki2, 1, :]
                        hvb = hv[:, kb, (2 * hp + 1) * 65:(2 * hp + 1) * 65 + 64]
                        hvo = hv[:, kb, (2 * hp + 1) * 65 + 64:
                                 (2 * hp + 1) * 65 + 65]
                        first = (gi == 0 and ki2 == 0)
                        last = (gi == ng - 1 and ki2 == n_in_g - 1)
                        mm(PD, hvb, rb, start=first, stop=last, waits=w0,
                           tp=(0, 64), out=PD.ap[64:128, :])
                        mm(PE, hvo, rb, start=first, stop=last, waits=[],
                           tp=(0, 32), out=PE.ap[32:33, :])
                    ptstate[slp] = b.now("t")
                qn_state["war"] = b.now("t")
                for hh in range(2):
                    srcap = PC.ap[64:65, :] if hh == 0 else PE.ap[32:33, :]
                    denap = den[64:65, :] if hh == 0 else den[32:33, :]
                    pvsl = PC if hh == 0 else PE
                    sA = b.op("scalar", lambda e, s_=srcap, d_=denap:
                              e.activation(out=d_, in_=s_, func=AF.Ln),
                              waits=[PC.prod if hh == 0 else PE.prod], inc="s")
                    sB = b.op("scalar", lambda e, hh=hh, d_=denap: e.activation(
                        out=recipb_r(hh), in_=d_, func=AF.Exp,
                        scale=-1.0), inc="s")
                    bsl = H[2 + hh]
                    onesr = ones[64:65, 0:64] if hh == 0 else ones[32:33, 0:64]
                    tp_b = (64, 0) if hh == 0 else (32, 64)
                    outp = bsl.ap[0:64, :] if hh == 0 else bsl.ap[64:128, :]
                    mm_group(bsl, [(onesr, recipb_r(hh), tp_b)],
                             waits=[("s", sB[1])], out=outp)
                    cbc = evac("vector", bsl,
                               (lambda e, hh=hh, bsl=bsl: e.tensor_copy(
                                   out=(bcast_sb[0:64, :] if hh == 0
                                        else bcast_sb[64:128, :]),
                                   in_=(bsl.ap[0:64, :] if hh == 0
                                        else bsl.ap[64:128, :]))))
                    pvv = PC.ap[0:64, :] if hh == 0 else PD.ap[64:128, :]
                    vout = (valsT[0:64, hp, :] if hh == 0
                            else valsT[64:128, hp, :])
                    bco = bcast_sb[0:64, :] if hh == 0 else bcast_sb[64:128, :]
                    capp = b.op("vector", lambda e, pvv=pvv, vout=vout, bco=bco:
                                e.tensor_tensor(out=vout, in0=pvv, in1=bco,
                                                op=ALU.mult),
                                waits=[("v", cbc[1]), vals_state[0]], inc="v")
                    if hh == 0:
                        PC.cons = capp
                    else:
                        PD.cons = capp
                        PE.cons = capp
                if li == 0 and j == 0 and hp in (0, 1, 2) and \
                        ("vu%d" % hp) in dbt:
                    allc = [b.now("v"), b.now("s"), b.now("t")]
                    b.dma("gpsimd", dbt["vu%d" % hp][:, :], valsT, waits=allc)
                if li == 0 and j == 0 and hp == 0 and "PT0" in dbt:
                    allc = [b.now("v"), b.now("s"), b.now("t")]
                    b.dma("gpsimd", dbt["PT0"][:, :], PT[:, :, :, :, :],
                          waits=allc)
                    b.dma("gpsimd", dbt["bpk0"][:, :], bpk[:, :], waits=allc)
            if li == 0 and j == 0 and "valsT0" in dbt:
                b.dma("gpsimd", dbt["valsT0"][:, :], valsT,
                      waits=[b.now("v"), b.now("s"), b.now("t")])
                b.dma("gpsimd", dbt["bpkd"][:, :], bpk[:, :],
                      waits=[b.now("v"), b.now("s"), b.now("t")])
                if "PTd" in dbt:
                    b.dma("gpsimd", dbt["PTd"][:, :], PT[:, :, :, :, :],
                          waits=[b.now("v"), b.now("s"), b.now("t")])
            if li == 0 and j == 0 and "vu0" in dbt and hp == 7:
                pass
            wname = "l0_wo" if li == 0 else "l1_outw"
            brow = bo_row if li == 0 else outb_row
            c_vals = b.now("v")
            for wf in range(2):
                sl4 = ((wf * 2, 0), (wf * 2, 1), (wf * 2 + 1, 0), (wf * 2 + 1, 1))
                for vc in range(NC):
                    cl = w_slice(dr[wname][:, vc * 128:(vc + 1) * 128].rearrange(
                        "(bb p) f -> p bb f", p=128))
                    cw = w_tr(cl)
                    for si, (tt, ii) in enumerate(sl4):
                        slot = H[si]
                        if vc == 0:
                            obase = 0 if li == 0 else 32
                            mm(slot, ones[obase:obase + 1, 0:128],
                               brow[:, ii * 512:(ii + 1) * 512],
                               start=True, stop=False,
                               waits=[slot.cons, c_bo, c_outb],
                               tp=(obase, 0))
                        mm(slot, valsT[:, vc, tt * 128:(tt + 1) * 128],
                           wtrb[:, ii * 512:(ii + 1) * 512],
                           start=False, stop=(vc == NC - 1),
                           waits=([cw, c_vals] if vc == 0 else [cw]))
                    wtr_consumed()
                for si, (tt, ii) in enumerate(sl4):
                    tg = j * 4 + tt
                    slot = H[si]
                    r = evac("vector", slot,
                             (lambda e, sl=slot, tg=tg, ii=ii:
                              e.scalar_tensor_tensor(
                                  out=x[:, tg, ii * 512:(ii + 1) * 512],
                                  in0=x[:, tg, ii * 512:(ii + 1) * 512],
                                  scalar=1.0, in1=sl.ap,
                                  op0=ALU.mult, op1=ALU.add)),
                             waits=[ready_x[tg]])
                    ready_x[tg] = ("v", r[1])
            vals_state[0] = b.now("t")

    lnr = ln_phase()
    pr = proj_to(dr["l0_wq"], list(range(NC)), lnr,
                 lambda oc, jj: (hT[:, oc, jj * 512:(jj + 1) * 512], "scalar"),
                 lambda oc: bqT[:, oc:oc + 1])
    for (oc, jj, r) in pr:
        trs([(PTB.ap[:, q * 128:(q + 1) * 128],
              hT[:, oc, (jj * 4 + q) * 128:(jj * 4 + q + 1) * 128])
             for q in range(4)], waits=[r])
        evac("vector", PTB, lambda e, oc=oc, jj=jj: e.tensor_copy(
            out=hvh[:, jj * 4:jj * 4 + 4, 2 * oc:2 * oc + 2, 0:64],
            in_=PTB.ap[:, 0:512].rearrange("p (k h c) -> p k h c", k=4, h=2)))
    c_h_all = b.now("s")
    c_hv_all = b.now("v")

    if limit == "l0proj":
        return _finish(b, dbt, dict(hT=hT, hv=hv, lnT=lnT, fpk=fpk), x, y_out)

    c_mxk = attn_prelude(0, c_h_all)
    if limit == "l0pre":
        return _finish(b, dbt, dict(fpk=fpk), x, y_out)
    attn_layer(0, lnr, c_h_all, c_hv_all, c_mxk)
    if limit == "l0attn":
        return _finish(b, dbt, dict(valsT=valsT, hT=hT, lnT=lnT), x, y_out)

    c_attn0_t = b.now("t")
    lnr = ln_phase()
    c_lnb_free = b.now("t")

    ce_prev, cmp2_prev = None, None
    mkp_war = [None]
    for t in range(NT):
        slot = H[t % 2]
        mm(slot, ones[64:65, 0:128], br_row, start=True, stop=False,
           waits=[slot.cons, c_br, c_wr], out=slot.ap[:, 0:8], tp=(64, 0))
        mm_group(slot, [(lnT[:, dc, t * 128:(t + 1) * 128], wr_t[:, dc, :], None)
                        for dc in range(NC)],
                 waits=[lnr[t // 4]], open_grp=False, out=slot.ap[:, 0:8])
        cm = b.op("vector", lambda e, sl=slot: e.reduce_max(
            out=rmax_c, in_=sl.ap[:, 0:8], axis=AX.X, negate=True),
            waits=[slot.prod, ce_prev], inc="v")
        ce = b.op("scalar", lambda e, sl=slot: e.activation(
            out=rexp_c, in_=sl.ap[:, 0:8], func=AF.Exp, bias=rmax_c, scale=1.0,
            accum_out=rden_c), waits=[("v", cm[1]), cmp2_prev], inc="s")
        cr = b.op("vector", lambda e: e.reciprocal(out=rrec_c, in_=rden_c),
                  waits=[("s", ce[1])], inc="v")
        cmask = b.op("vector", lambda e, sl=slot, t=t: e.tensor_scalar(
            out=rexp_c, in0=sl.ap[:, 0:8],
            scalar1=rmax_c, scalar2=0.0, op0=ALU.add, op1=ALU.is_ge),
            waits=[("v", cr[1]), ("s", ce[1])], inc="v")
        cmp2 = b.op("vector", lambda e, t=t: e.tensor_scalar(
            out=maskp[:, t * 8:(t + 1) * 8], in0=rexp_c,
            scalar1=rrec_c, scalar2=None, op0=ALU.mult),
            waits=[("t", c_lnb_free[1])], inc="v")
        ce_prev, cmp2_prev = ("s", ce[1]), ("v", cmp2[1])
        slot.cons = cmask
    c_maskp = b.now("v")
    if "mkpT2" in dbt:
        b.dma("gpsimd", dbt["mkpT2"][0:8, :], maskpT,
              waits=[b.now("v"), b.now("s"), b.now("t")])

    def hTe(jc):
        return hTeA[:, jc, :] if jc < 10 else hTeC[:, jc - 10, :]

    def hl1(hc):
        return hTl1A[:, hc, :] if hc < 20 else hTl1C[:, hc - 20, :]

    ytmp_war = [None, None]
    b2stage_war = [None]
    order = [(e, jc) for e in range(8) for jc in range(16)]
    c_w1 = {}
    w1cons = {0: None, 1: None}

    def load_w1(idx):
        e_i, jc = order[idx]
        c_w1[idx] = b.dma(
            "gpsimd", w1sl[:, idx % 2, :, :],
            dr["l0_w1"][e_i, :, jc * 128:(jc + 1) * 128].rearrange(
                "(dc p) f -> p dc f", p=128),
            waits=[w1cons[idx % 2], c_attn0_t])

    load_w1(0); load_w1(1)
    w2state = {}

    def load_w2(e_i, dh, jc):
        s = (e_i * 32 + dh * 16 + jc) % 18
        c = b.dma("gpsimd", w2sl[:, s, :],
                  dr["l0_w2"][e_i, jc * 128:(jc + 1) * 128,
                              dh * 512:(dh + 1) * 512],
                  waits=[w2state.get(s), c_attn0_t])
        return s, c

    for e_i in range(8):
        for jc in range(16):
            idx = e_i * 16 + jc
            for jj in range(NJ):
                slot = H[jj % 4]
                mm_group(slot, [(w1sl[:, idx % 2, dc, :],
                                 lnT[:, dc, jj * 512:(jj + 1) * 512], None)
                                for dc in range(NC)],
                         waits=[c_w1[idx], lnr[jj]])
                bia = fpk[:, 96 + jc * 8 + e_i:97 + jc * 8 + e_i]
                if jc % 2 == 0:
                    evac("scalar", slot,
                         (lambda e, sl=slot, jc=jc, jj=jj, bia=bia:
                          e.activation(out=hTe(jc)[:, jj * 512:(jj + 1) * 512],
                                       in_=sl.ap, func=AF.Relu, bias=bia,
                                       scale=1.0)))
                else:
                    evac("vector", slot,
                         (lambda e, sl=slot, jc=jc, jj=jj, bia=bia:
                          e.tensor_scalar(
                              out=hTe(jc)[:, jj * 512:(jj + 1) * 512],
                              in0=sl.ap, scalar1=bia, scalar2=0.0,
                              op0=ALU.add, op1=ALU.max)))
            w1cons[idx % 2] = b.now("t")
            if idx + 2 < 128:
                load_w1(idx + 2)
        c_h_s, c_h_v = b.now("s"), b.now("v")
        c_b2s = b.dma("gpsimd", bo_row, b2_rows[e_i:e_i + 1, :],
                      waits=[b2stage_war[0], c_b2])
        for dh in range(2):
            w2c = [load_w2(e_i, dh, jc) for jc in range(16)]
            for t128 in range(NT):
                slot = H[t128 % 4]
                mm_group(slot, [(hTe(jc)[:, t128 * 128:(t128 + 1) * 128],
                                 w2sl[:, w2c[jc][0], :], None)
                                for jc in range(16)],
                         waits=[w2c[15][1], c_h_s, c_h_v], close_grp=False)
                mm(slot, ones[0:1, 0:128], bo_row[0:1, dh * 512:(dh + 1) * 512],
                   start=False, stop=True, waits=[c_b2s])
                par = t128 % 2
                rs = evac("scalar", slot,
                          (lambda e, sl=slot, t128=t128, e_i=e_i, par=par:
                           e.activation(out=ytmp[:, par, :], in_=sl.ap,
                                        func=AF.Identity,
                                        scale=maskp[:, t128 * 8 + e_i:
                                                    t128 * 8 + e_i + 1])),
                          waits=[c_maskp, ytmp_war[par]])
                if only_e is not None and e_i != only_e:
                    slot.cons = rs
                    continue
                r = b.op("vector",
                         (lambda e, t128=t128, dh=dh, par=par:
                          e.tensor_tensor(
                              out=x[:, t128, dh * 512:(dh + 1) * 512],
                              in0=x[:, t128, dh * 512:(dh + 1) * 512],
                              in1=ytmp[:, par, :], op=ALU.add)),
                         waits=[("s", rs[1]), ready_x[t128]], inc="v")
                ytmp_war[par] = ("v", r[1])
                ready_x[t128] = ("v", r[1])
            cend = b.now("t")
            b2stage_war[0] = cend
            for jc in range(16):
                w2state[w2c[jc][0]] = cend
        if ("xaf%d" % e_i) in dbt:
            b.dma("gpsimd", dbt["xaf%d" % e_i][:, :], x[:, :, :],
                  waits=[b.now("v"), b.now("s"), b.now("t")])

    if limit == "l0":
        return _finish(b, dbt, dict(maskp=maskp, lnT=lnT, hTeA=hTeA,
                                    hTeC=hTeC), x, y_out)

    c_ffn0_t = b.now("t")
    c_ffn0_v = b.now("v")
    lnr = ln_phase(guard_s=c_ffn0_v, guard_v=c_ffn0_v)
    c_hvones = b.op("gpsimd", lambda e: e.memset(hvh[:, :, :, 64:65], 1.0),
                    waits=[c_ffn0_t], inc="g")
    ptstate[0] = c_ffn0_t
    ptstate[1] = c_ffn0_t
    vals_state[0] = c_ffn0_t

    vstg_free = [lnb_free[0], lnb_free[0]]
    kres = proj_to(dr["l1_qkvw"], list(range(8, 16)), lnr,
                   lambda oc, jj: (kT[:, oc - 8, jj * 512:(jj + 1) * 512],
                                   "scalar"),
                   lambda oc: qkvbT[:, oc:oc + 1], extra=c_ffn0_t)
    c_kT = b.now("s")
    for oc in range(16, 24):
        cl = w_slice(dr["l1_qkvw"][oc * 128:(oc + 1) * 128, :], extra=c_ffn0_t)
        cw = w_tr(cl)
        for jj in range(NJ):
            slot = H[jj % 4]
            mm_group(slot, [(wtrb[:, dc * 128:(dc + 1) * 128],
                             lnT[:, dc, jj * 512:(jj + 1) * 512], None)
                            for dc in range(NC)],
                     waits=[cw, lnr[jj]])
            par = jj % 2
            rv = evac("scalar", slot,
                      (lambda e, sl=slot, par=par, oc=oc:
                       e.activation(out=lnb[:, par * 512:(par + 1) * 512],
                                    in_=sl.ap, func=AF.Identity,
                                    bias=qkvbT[:, oc:oc + 1], scale=1.0)),
                      waits=[("v", c_ffn0_v[1]), vstg_free[par]])
            rtv = trs([(PTB.ap[:, q * 128:(q + 1) * 128],
                  lnb[:, par * 512 + q * 128:par * 512 + (q + 1) * 128])
                 for q in range(4)], waits=[rv])
            vstg_free[par] = ("t", rtv[1])
            vc = oc - 16
            evac("vector", PTB, lambda e, vc=vc, jj=jj: e.tensor_copy(
                out=hvh[:, jj * 4:jj * 4 + 4, 2 * vc:2 * vc + 2, 0:64],
                in_=PTB.ap[:, 0:512].rearrange("p (k h c) -> p k h c",
                                               k=4, h=2)))
        wtr_consumed()
    c_hv1 = b.now("v")

    c_mxk = attn_prelude(1, c_kT)
    attn_layer(1, lnr, c_kT, c_hv1, c_mxk)
    if limit == "l1attn":
        return _finish(b, dbt, dict(valsT=valsT, qTj=qTj), x, y_out)

    c_attn1_t = b.now("t")
    lnr = ln_phase(guard_s=("t", c_attn1_t[1]), guard_v=("t", c_attn1_t[1]))
    gustate = {"cons": None, "wtr_cons": None, 0: None, 1: None}
    wdstate = {}

    def gu_slice(dram, oc, par, extra=None):
        c = b.dma("gpsimd", guld[:, par, :], dram[oc * 128:(oc + 1) * 128, :],
                  waits=[gustate[par], c_attn1_t, extra])
        return c

    def gu_tr(c_load, par):
        r = trs([(PTB.ap[:, i * 128:(i + 1) * 128],
                  guld[:, par, i * 128:(i + 1) * 128]) for i in range(NC)],
                waits=[c_load])
        gustate[par] = ("t", r[1])
        ev = evac("vector", PTB,
                  lambda e, par=par: e.tensor_copy(out=gutr[:, par, :],
                                                   in_=PTB.ap[:, :]),
                  waits=[gustate["wtr_cons"]])
        return ev

    silu_war = [None, None]
    for th in range(2):
        for hc in range(24):
            cg = gu_slice(dr["l1_wg"], hc, 0)
            cwg = gu_tr(cg, 0)
            cu = gu_slice(dr["l1_wu"], hc, 1)
            cwu = gu_tr(cu, 1)
            for jl in range(2):
                jj = th * 2 + jl
                sg, su = H[jl * 2], H[jl * 2 + 1]
                mm_group(sg, [(gutr[:, 0, dc * 128:(dc + 1) * 128],
                               lnT[:, dc, jj * 512:(jj + 1) * 512], None)
                              for dc in range(NC)], waits=[cwg, lnr[jj]])
                rs = evac("scalar", sg,
                          (lambda e, sl=sg, jl=jl: e.activation(
                              out=silu[:, jl, :], in_=sl.ap, func=AF.Silu,
                              scale=1.0 / 127.0)),
                          waits=[("t", c_attn1_t[1]), silu_war[jl]])
                mm_group(su, [(gutr[:, 1, dc * 128:(dc + 1) * 128],
                               lnT[:, dc, jj * 512:(jj + 1) * 512], None)
                              for dc in range(NC)], waits=[cwu, lnr[jj]])
                rv = evac("vector", su,
                          (lambda e, sl=su, hc=hc, jl=jl: e.scalar_tensor_tensor(
                              out=hl1(hc)[:, jl * 512:(jl + 1) * 512],
                              in0=sl.ap, scalar=1.0 / (127.0 * 127.0), in1=silu[:, jl, :],
                              op0=ALU.mult, op1=ALU.mult)),
                          waits=[("s", rs[1]), ("t", c_attn1_t[1])])
                silu_war[jl] = ("v", rv[1])
            gustate["wtr_cons"] = b.now("t")
        c_h1_s, c_h1_v = b.now("s"), b.now("v")
        for dh in range(2):
            wdc = []
            for hc in range(24):
                s = (th * 48 + dh * 24 + hc) % 26
                c = b.dma("gpsimd", wldb[:, :],
                          dr["l1_wd"][:, hc * 128:(hc + 1) * 128].rearrange(
                              "(bb p) f -> p bb f", p=128),
                          waits=[wstate["cons"], c_attn1_t])
                rt = trs([(PTB.ap[:, i * 128:(i + 1) * 128],
                           wldb[:, i * 128:(i + 1) * 128]) for i in range(NC)],
                         waits=[c])
                wstate["cons"] = ("t", rt[1])
                ev = evac("vector", PTB,
                          (lambda e, s=s, dh=dh: e.tensor_copy(
                              out=wdsl[:, s, :],
                              in_=PTB.ap[:, dh * 512:(dh + 1) * 512])),
                          waits=[wdstate.get(s)])
                wdc.append((s, ev))
            for tl in range(8):
                t128 = th * 8 + tl
                slot = H[tl % 4]
                mm_group(slot, [(hl1(hc)[:, tl * 128:(tl + 1) * 128],
                                 wdsl[:, wdc[hc][0], :], None)
                                for hc in range(24)],
                         waits=[wdc[23][1], c_h1_s, c_h1_v])
                r = evac("vector", slot,
                         (lambda e, sl=slot, t128=t128, dh=dh:
                          e.scalar_tensor_tensor(
                              out=x[:, t128, dh * 512:(dh + 1) * 512],
                              in0=x[:, t128, dh * 512:(dh + 1) * 512],
                              scalar=1.0, in1=sl.ap,
                              op0=ALU.mult, op1=ALU.add)),
                         waits=[ready_x[t128]])
                ready_x[t128] = ("v", r[1])
            cend = b.now("t")
            for s, _ in wdc:
                wdstate[s] = cend

    return _finish(b, dbt, dict(hTl1=hTl1A), x, y_out)


def _finish(b, dbt, named, x, y_out):
    allw = [b.now("t"), b.now("s"), b.now("v"), b.now("g")]
    for name, ap in named.items():
        if name in dbt:
            b.dma("gpsimd", dbt[name][:, :], ap, waits=allw)
    b.dma("sync", y_out.rearrange("(t p) d -> p t d", p=128), x[:, :, :],
          waits=allw)
    b.force_wait("sync", "xq", b.cnt["xq"])
    b.force_wait("gpsimd", "w", b.cnt["w"])
    b.emit()
    return True


from concourse.bass_utils import run_bass_kernel_spmd

_IN_MAP = {
    "x_in": "L0_attn_q", "l0_wq": "L0_attn_Wq", "l0_bq": "L0_attn_bq",
    "l0_wo": "L0_attn_Wo", "l0_bo": "L0_attn_bo", "l0_wr": "L0_ffn_Wr",
    "l0_br": "L0_ffn_br", "l0_w1": "L0_ffn_W1", "l0_b1": "L0_ffn_b1",
    "l0_w2": "L0_ffn_W2", "l0_b2": "L0_ffn_b2", "l1_qkvw": "L1_attn_qkv_w",
    "l1_qkvb": "L1_attn_qkv_b", "l1_outw": "L1_attn_out_w",
    "l1_outb": "L1_attn_out_b", "l1_wg": "L1_ffn_w_gate",
    "l1_wu": "L1_ffn_w_up", "l1_wd": "L1_ffn_w_down",
}


def kernel(**inputs):
    nc = bass.Bass("TRN2", target_bir_lowering=False, debug=False)
    build(nc, limit="all")
    in_maps = []
    for core in range(8):
        m = {}
        for k, rk in _IN_MAP.items():
            v = inputs[rk]
            if k == "x_in":
                v = v[core]
            m[k] = np.ascontiguousarray(v)
        in_maps.append(m)
    res = run_bass_kernel_spmd(nc, in_maps, core_ids=list(range(8)))
    return np.stack([res.results[c]["y_out"] for c in range(8)])
